# version 01
# baseline (speedup 1.0000x reference)
import numpy as np
import concourse.bass as bass
import concourse.mybir as mybir
from concourse.bass_utils import run_bass_kernel_spmd
from contextlib import ExitStack

F32 = mybir.dt.float32
BF16 = mybir.dt.bfloat16
ALU = mybir.AluOpType
ACT = mybir.ActivationFunctionType
AX = mybir.AxisListType
MAGIC = 12582912.0
EPS = 1e-5


def build_nc(D, T, H, NT):
    assert T == D, "layout reuse assumes T == D"
    KD = D // 128
    KT = T // 128
    KH = H // 128
    GT = T // NT
    GH = H // NT
    GE = D // NT
    TH = T // 2
    KTH = TH // 128
    HH = H // 2
    WCH = min(512, H // 2, D // 2)
    SCALE = float(128.0 ** -0.5)
    inv_d = 1.0 / D

    nc = bass.Bass("TRN2", target_bir_lowering=False)

    x = nc.dram_tensor("x", [T, D], F32, kind="ExternalInput").ap()
    wq = nc.dram_tensor("wq", [KD, 128, D], F32, kind="ExternalInput").ap()
    wk = nc.dram_tensor("wk", [KD, 128, D], F32, kind="ExternalInput").ap()
    wv = nc.dram_tensor("wv", [KD, 128, D], F32, kind="ExternalInput").ap()
    wo = nc.dram_tensor("wo", [KD, 128, D], F32, kind="ExternalInput").ap()
    bqkv = nc.dram_tensor("bqkv", [128, 3 * KD], F32, kind="ExternalInput").ap()
    bo_rep = nc.dram_tensor("bo_rep", [128, D], F32, kind="ExternalInput").ap()
    wup = nc.dram_tensor("wup", [KD, 128, H], F32, kind="ExternalInput").ap()
    wdn = nc.dram_tensor("wdn", [KH, 128, D], F32, kind="ExternalInput").ap()
    psel = nc.dram_tensor("psel", [128, 8], F32, kind="ExternalInput").ap()
    out = nc.dram_tensor("out", [T, D], F32, kind="ExternalOutput").ap()

    d_xln = nc.dram_tensor("d_xln", [T, D], BF16, kind="Internal").ap()
    d_kvs = nc.dram_tensor("d_kvs", [KD, 128, T], F32, kind="Internal").ap()
    d_r1 = nc.dram_tensor("d_r1", [T, D], F32, kind="Internal").ap()
    d_nup = nc.dram_tensor("d_nup", [T, D], BF16, kind="Internal").ap()
    d_relu = nc.dram_tensor("d_relu", [T, H], F32, kind="Internal").ap()
    d_ndn = nc.dram_tensor("d_ndn", [T, H], BF16, kind="Internal").ap()
    d_wupq = nc.dram_tensor("d_wupq", [KD, 128, H], BF16, kind="Internal").ap()
    d_wdnq = nc.dram_tensor("d_wdnq", [KH, 128, D], BF16, kind="Internal").ap()
    d_carry = nc.dram_tensor("d_carry", [128, KD], F32, kind="Internal").ap()
    d_call = nc.dram_tensor("d_call", [8, 128, KD], F32, kind="Internal").ap()

    ctx = ExitStack()

    def mk_sb(stack, name, shape, dt=F32):
        return stack.enter_context(nc.sbuf_tensor(name, shape, dt))

    def sb(name, shape, dt=F32):
        return mk_sb(ctx, name, shape, dt)

    s_sum = sb("s_sum", [128, KT]); s_ssq = sb("s_ssq", [128, KT])
    s_rstd = sb("s_rstd", [128, KT]); s_nmr = sb("s_nmr", [128, KT])
    s_sc1 = sb("s_sc1", [128, KT]); s_sc2 = sb("s_sc2", [128, KT])
    s_c2 = sb("s_c2", [128, KT]); s_s2 = sb("s_s2", [128, KT])
    s_maxr = sb("s_maxr", [128, KT]); s_k2 = sb("s_k2", [128, KT])
    s_q3 = sb("s_q3", [128, KT]); s_k3 = sb("s_k3", [128, KT])
    s_big = sb("s_big", [128, KD * T], BF16)
    s_ctx = sb("s_ctx", [128, KD * T], BF16)
    s_carry = sb("s_carry", [128, KD])
    s_crecv = sb("s_crecv", [128, 8 * KD])
    s_rcm = [sb(f"s_rcm{i}", [128, KD]) for i in range(2)]
    s_psel = sb("s_psel", [128, 8])
    s_bqkv = sb("s_bqkv", [128, 3 * KD])
    s_one = sb("s_one", [128, 1])
    s_nmg = sb("s_nmg", [128, 1])
    s_row = sb("s_row", [1, 128])
    s_red = sb("s_red", [128, 1])
    s_red2 = sb("s_red2", [128, 1])
    s_swu = sb("s_swu", [128, 1]); s_swd = sb("s_swd", [128, 1])
    s_iwu = sb("s_iwu", [128, 1]); s_iwd = sb("s_iwd", [128, 1])
    s_wsum = sb("s_wsum", [128, 2])

    g_pw = ExitStack()
    s_wfl = [mk_sb(g_pw, f"s_wfl{i}", [128, WCH]) for i in range(2)]
    s_wq8 = [mk_sb(g_pw, f"s_wq8{i}", [128, WCH], BF16) for i in range(2)]

    ps_a = [ctx.enter_context(nc.psum_tensor(f"ps_a{i}", [128, NT], F32)) for i in range(4)]
    ps_b = [ctx.enter_context(nc.psum_tensor(f"ps_b{i}", [128, NT], F32)) for i in range(2)]
    ps_o = [ctx.enter_context(nc.psum_tensor(f"ps_o{i}", [128, NT], F32)) for i in range(2)]

    class Sem:
        __slots__ = ("h", "v")
        def __init__(self, name):
            self.h = ctx.enter_context(nc.semaphore(name))
            self.v = 0

    plan = {"sync": [], "gpsimd": [], "vector": [], "scalar": [], "tensor": []}

    def emit(eng, fn):
        plan[eng].append(fn)

    def wait(eng, sem, val):
        if val <= 0:
            return
        emit(eng, lambda e, h=sem.h, v=val: e.wait_ge(h, v))

    def dma(eng, sem, outap, inap, transpose=False):
        sem.v += 16
        if transpose:
            emit(eng, lambda e, o=outap, i=inap, h=sem.h: e.dma_start(out=o, in_=i, transpose=True).then_inc(h, 16))
        else:
            emit(eng, lambda e, o=outap, i=inap, h=sem.h: e.dma_start(out=o, in_=i).then_inc(h, 16))
        return sem.v

    AUTOF = ("vector", "scalar")

    def op(eng, sem, fn):
        sem.v += 1
        v = sem.v
        emit(eng, lambda e, f=fn, h=sem.h: f(e).then_inc(h, 1))
        if eng in AUTOF:
            emit(eng, lambda e, h=sem.h, v=v: e.wait_ge(h, v))
        return v

    def raw(eng, fn):
        emit(eng, fn)

    sv = Sem("sv"); sa = Sem("sa"); pe = Sem("pe")
    sg = Sem("sg"); sw = Sem("sw"); sr = Sem("sr"); so = Sem("so")
    scc = Sem("scc")
    sx = [Sem("sx0"), Sem("sx1")]
    sgk = [Sem("sgk0"), Sem("sgk1")]
    sgq = [Sem("sgq0"), Sem("sgq1")]
    sgu = [Sem("sgu0"), Sem("sgu1")]
    sgd = [Sem("sgd0"), Sem("sgd1")]
    ql = [Sem("ql0"), Sem("ql1")]
    qsu = [Sem("qsu0"), Sem("qsu1")]
    qsd = [Sem("qsd0"), Sem("qsd1")]
    swx = [Sem("swx0"), Sem("swx1")]
    swn = [Sem("swn0"), Sem("swn1")]
    swr = [Sem("swr0"), Sem("swr1")]
    swl = [Sem("swl0"), Sem("swl1")]
    swd = [Sem("swd0"), Sem("swd1")]
    srl = [Sem("srl0"), Sem("srl1")]
    srr = [Sem("srr0"), Sem("srr1")]
    swc = Sem("swc"); src8 = Sem("src8"); srk = Sem("srk")

    def _chunks(kind):
        KN = KD if kind == "u" else KH
        NCOL = H if kind == "u" else D
        return [(kind, k, c) for k in range(KN) for c in range(0, NCOL, WCH)]

    w1_tiles = _chunks("u") + _chunks("d")
    w2_tiles = list(w1_tiles)
    w1_guard = [0, 0]; w1_idx = [0]
    w2_idx = [0]; w2_guard = [0, 0]; w2_a_guard = [0, 0]
    w2_ready = [0]
    w2_w_guard = [0, 0]; w2_w_sem = [None, None]

    def w1_task():
        if w1_idx[0] >= len(w1_tiles):
            return
        kind, k, c = w1_tiles[w1_idx[0]]
        i = w1_idx[0]; w1_idx[0] += 1
        p = i % 2
        srcp = (wup if kind == "u" else wdn)[k][:, c:c + WCH]
        wait("scalar", sv, w1_guard[p])
        ld = dma("scalar", ql[p], s_wfl[p][:], srcp)
        wait("vector", ql[p], ld)
        op("vector", sv, lambda e, p=p: e.tensor_reduce(
            s_red[:], s_wfl[p][:], AX.X, ALU.add, apply_absolute_value=True))
        col = 0 if kind == "u" else 1
        if (kind == "u" and i == 0) or (kind == "d" and k == 0 and c == 0):
            op("vector", sv, lambda e, col=col: e.tensor_copy(s_wsum[:, col:col + 1], s_red[:]))
        else:
            op("vector", sv, lambda e, col=col: e.tensor_tensor(s_wsum[:, col:col + 1], s_wsum[:, col:col + 1], s_red[:], ALU.add))
        w1_guard[p] = sv.v

    def w2_task():
        if w2_idx[0] >= len(w2_tiles) or w2_ready[0] == 0:
            return
        kind, k, c = w2_tiles[w2_idx[0]]
        i = w2_idx[0]; w2_idx[0] += 1
        p = i % 2
        srcp = (wup if kind == "u" else wdn)[k][:, c:c + WCH]
        dstp = (d_wupq if kind == "u" else d_wdnq)[k][:, c:c + WCH]
        inv = s_iwu if kind == "u" else s_iwd
        wait("scalar", sv, max(w2_guard[p], w1_guard[p]))
        wait("scalar", sa, w2_a_guard[p])
        ld = dma("scalar", ql[p], s_wfl[p][:], srcp)
        wait("vector", ql[p], ld)
        wait("vector", sv, w2_ready[0])
        op("vector", sv, lambda e, p=p, inv=inv: e.tensor_scalar(
            s_wfl[p][:], s_wfl[p][:], inv[:, 0:1], 1.0, ALU.mult, ALU.min))
        op("vector", sv, lambda e, p=p: e.tensor_scalar(
            s_wfl[p][:], s_wfl[p][:], -1.0, MAGIC, ALU.max, ALU.add))
        w2_guard[p] = sv.v
        wait("scalar", sv, w2_guard[p])
        if w2_w_sem[p] is not None:
            wait("scalar", w2_w_sem[p], w2_w_guard[p])
        op("scalar", sa, lambda e, p=p: e.activation(
            s_wq8[p][:], s_wfl[p][:], ACT.Identity, bias=s_nmg[:, 0:1]))
        w2_a_guard[p] = sa.v
        wait("scalar", sa, w2_a_guard[p])
        qs = (qsu if kind == "u" else qsd)[p]
        dma("scalar", qs, dstp, s_wq8[p][:])
        w2_w_sem[p] = qs
        w2_w_guard[p] = qs.v

    g_p0 = ExitStack()
    s_x = [mk_sb(g_p0, f"s_x{i}", [128, D]) for i in range(2)]
    s_sq = [mk_sb(g_p0, f"s_sq{i}", [128, D]) for i in range(2)]
    s_xln = [mk_sb(g_p0, f"s_xln{i}", [128, D], BF16) for i in range(2)]

    dma("sync", sr, s_psel[:], psel)
    dma("sync", sr, s_bqkv[:], bqkv)
    op("vector", sv, lambda e: e.memset(s_nmg[:], -MAGIC))
    op("vector", sv, lambda e: e.memset(s_one[:], 1.0))
    op("vector", sv, lambda e: e.memset(s_row[:], 1.0))
    op("vector", sv, lambda e: e.memset(s_maxr[:], 0.0))

    stats_done = [0, 0]
    sq_done = [0, 0]
    for i in range(KT):
        p = i % 2
        wait("sync", sv, stats_done[p])
        dma("sync", sx[p], s_x[p][:], x[i * 128:(i + 1) * 128, :])
        wait("vector", sx[p], sx[p].v)
        wait("vector", sv, sq_done[p])
        op("vector", sv, lambda e, p=p: e.tensor_tensor(s_sq[p][:], s_x[p][:], s_x[p][:], ALU.mult))
        op("vector", sv, lambda e, p=p, i=i: e.tensor_reduce(s_ssq[:, i:i + 1], s_sq[p][:], AX.X, ALU.add))
        sq_done[p] = sv.v
        op("vector", sv, lambda e, p=p, i=i: e.tensor_reduce(s_sum[:, i:i + 1], s_x[p][:], AX.X, ALU.add))
        stats_done[p] = sv.v
        w1_task(); w1_task()

    op("vector", sv, lambda e: e.tensor_scalar(s_sum[:], s_sum[:], inv_d, None, ALU.mult))
    op("vector", sv, lambda e: e.tensor_scalar(s_ssq[:], s_ssq[:], inv_d, None, ALU.mult))
    op("vector", sv, lambda e: e.tensor_tensor(s_sc1[:], s_sum[:], s_sum[:], ALU.mult))
    op("vector", sv, lambda e: e.tensor_tensor(s_sc1[:], s_ssq[:], s_sc1[:], ALU.subtract))
    op("vector", sv, lambda e: e.tensor_scalar(s_sc1[:], s_sc1[:], EPS, None, ALU.add))
    op("vector", sv, lambda e: e.reciprocal(s_sc2[:], s_sc1[:]))
    va_done = sv.v
    wait("scalar", sv, va_done)
    op("scalar", sa, lambda e: e.activation(s_rstd[:], s_sc2[:], ACT.Sqrt))
    wait("vector", sa, sa.v)

    def newton(dst_ap, va_ap, scr_ap, n=2):
        for _ in range(n):
            op("vector", sv, lambda e: e.tensor_tensor(scr_ap, dst_ap, dst_ap, ALU.mult))
            op("vector", sv, lambda e: e.tensor_tensor(scr_ap, scr_ap, va_ap, ALU.mult))
            op("vector", sv, lambda e: e.tensor_scalar(scr_ap, scr_ap, -0.5, 1.5, ALU.mult, ALU.add))
            op("vector", sv, lambda e: e.tensor_tensor(dst_ap, dst_ap, scr_ap, ALU.mult))

    newton(s_rstd[:], s_sc1[:], s_sc2[:])
    op("vector", sv, lambda e: e.tensor_tensor(s_nmr[:], s_sum[:], s_rstd[:], ALU.mult))
    op("vector", sv, lambda e: e.tensor_scalar(s_nmr[:], s_nmr[:], -1.0, None, ALU.mult))
    ln1_done = sv.v

    wait("scalar", sv, ln1_done)
    wait("sync", sv, ln1_done)
    apply_done = [0, 0]
    xlnw_done = [0, 0]
    for i in range(KT):
        p = i % 2
        wait("sync", sa, apply_done[p])
        wait("sync", sx[p], sx[p].v)
        dma("sync", sx[p], s_x[p][:], x[i * 128:(i + 1) * 128, :])
        wait("scalar", sx[p], sx[p].v)
        wait("scalar", swx[p], xlnw_done[p])
        op("scalar", sa, lambda e, p=p, i=i: e.activation(
            s_xln[p][:], s_x[p][:], ACT.Identity, bias=s_nmr[:, i:i + 1], scale=s_rstd[:, i:i + 1]))
        apply_done[p] = sa.v
        wait("sync", sa, apply_done[p])
        dma("sync", swx[p], d_xln[i * 128:(i + 1) * 128, :], s_xln[p][:])
        xlnw_done[p] = swx[p].v
        w1_task()

    wait("sync", swx[0], swx[0].v)
    wait("sync", swx[1], swx[1].v)
    for kd in range(KD):
        dma("sync", sr, s_big[:, kd * T:(kd + 1) * T],
            d_xln[:, kd * 128:(kd + 1) * 128], transpose=True)
    r_xlnT = sr.v

    g_p0.close()
    g_ab = ExitStack()
    s_wk = [mk_sb(g_ab, f"s_wk{i}", [128, D], BF16) for i in range(2)]
    s_wv = [mk_sb(g_ab, f"s_wv{i}", [128, D], BF16) for i in range(2)]
    s_wq = [mk_sb(g_ab, f"s_wq{i}", [128, D], BF16) for i in range(2)]
    b1 = mk_sb(g_ab, "b1", [128, T]); b2 = mk_sb(g_ab, "b2", [128, T])
    b3 = mk_sb(g_ab, "b3", [128, T])

    p0_v, p0_a = sv.v, sa.v
    wait("gpsimd", sv, p0_v)
    wait("gpsimd", sa, p0_a)
    wait("gpsimd", swx[0], swx[0].v)
    wait("gpsimd", swx[1], swx[1].v)
    wait("tensor", sr, sr.v)
    kv_w_done = [0, 0]
    kv_pe_done = [0, 0]
    a_bank_free = {}
    kvsw_done = 0
    for ke in range(KD):
        p = ke % 2
        wait("gpsimd", pe, kv_pe_done[p])
        dma("gpsimd", sgk[p], s_wk[p][:], wk[ke])
        dma("gpsimd", sgk[p], s_wv[p][:], wv[ke])
        kv_w_done[p] = sgk[p].v
        wait("tensor", sgk[p], kv_w_done[p])
        wait("scalar", sw, kvsw_done)
        for g in range(GT):
            bank = g % 2
            if ("k", bank) in a_bank_free:
                wait("tensor", sa, a_bank_free[("k", bank)])
            for kd in range(KD):
                fn = (lambda e, p=p, kd=kd, g=g, bank=bank: e.matmul(
                    ps_a[bank][:], s_wk[p][:, kd * 128:(kd + 1) * 128],
                    s_big[:, kd * T + g * NT: kd * T + g * NT + NT],
                    start=(kd == 0), stop=(kd == KD - 1)))
                if kd == KD - 1:
                    op("tensor", pe, fn)
                else:
                    raw("tensor", fn)
            mm_done = pe.v
            gs = slice(g * NT, g * NT + NT)
            wait("scalar", pe, mm_done)
            op("scalar", sa, lambda e, bank=bank, gs=gs, ke=ke: e.activation(
                b1[:, gs], ps_a[bank][:], ACT.Identity, bias=s_bqkv[:, KD + ke:KD + ke + 1]))
            a_bank_free[("k", bank)] = sa.v
            wait("vector", sa, sa.v)
            op("vector", sv, lambda e, gs=gs: e.tensor_single_scalar(b2[:, gs], b1[:, gs], 0.0, ALU.min))
            wait("scalar", sv, sv.v)
            op("scalar", sa, lambda e, gs=gs: e.activation(b3[:, gs], b2[:, gs], ACT.Exp))
            wait("vector", sa, sa.v)
            op("vector", sv, lambda e, gs=gs: e.scalar_tensor_tensor(
                b2[:, gs], b1[:, gs], 0.0, b3[:, gs], ALU.max, ALU.add))
        for g in range(GT):
            bank = 2 + (g % 2)
            if ("v", bank) in a_bank_free:
                wait("tensor", sv, a_bank_free[("v", bank)])
            for kd in range(KD):
                fn = (lambda e, p=p, kd=kd, g=g, bank=bank: e.matmul(
                    ps_a[bank][:], s_wv[p][:, kd * 128:(kd + 1) * 128],
                    s_big[:, kd * T + g * NT: kd * T + g * NT + NT],
                    start=(kd == 0), stop=(kd == KD - 1)))
                if kd == KD - 1:
                    op("tensor", pe, fn)
                else:
                    raw("tensor", fn)
            mm_done = pe.v
            gs = slice(g * NT, g * NT + NT)
            wait("vector", pe, mm_done)
            op("vector", sv, lambda e, bank=bank, gs=gs, ke=ke: e.scalar_tensor_tensor(
                b1[:, gs], ps_a[bank][:], s_bqkv[:, 2 * KD + ke:2 * KD + ke + 1], b2[:, gs],
                ALU.add, ALU.mult))
            a_bank_free[("v", bank)] = sv.v
        wait("vector", sw, kvsw_done)
        op("vector", sv, lambda e: e.tensor_tensor_scan(
            b3[:], b1[:], b1[:], 0.0, ALU.add, ALU.bypass))
        op("vector", sv, lambda e, ke=ke: e.tensor_copy(s_carry[:, ke:ke + 1], b3[:, T - 1:T]))
        scan_v = sv.v
        wait("sync", sv, scan_v)
        dma("sync", sw, d_kvs[ke], b3[:])
        kvsw_done = sw.v
        kv_pe_done[p] = pe.v
        for _ in range(8):
            w1_task()
    a_done_w = sw.v
    a_carry_v = sv.v

    wait("sync", sv, a_carry_v)
    dma("sync", swc, d_carry, s_carry[:])
    wait("gpsimd", swc, swc.v)
    scc.v += 1
    emit("gpsimd", lambda e, h=scc.h: e.collective_compute(
        "AllGather", ALU.bypass, replica_groups=[[0, 1, 2, 3, 4, 5, 6, 7]],
        ins=[d_carry], outs=[d_call]).then_inc(h, 1))
    wait("sync", scc, scc.v)
    for r in range(8):
        dma("sync", src8, s_crecv[:, r * KD:(r + 1) * KD], d_call[r])
    wait("vector", src8, src8.v)
    op("vector", sv, lambda e: e.tensor_scalar(s_rcm[0][:], s_crecv[:, 0:KD], s_psel[:, 0:1], None, ALU.mult))
    for r in range(1, 8):
        cur, nxt = s_rcm[(r - 1) % 2], s_rcm[r % 2]
        op("vector", sv, lambda e, r=r, cur=cur, nxt=nxt: e.scalar_tensor_tensor(
            nxt[:], s_crecv[:, r * KD:(r + 1) * KD], s_psel[:, r:r + 1], cur[:], ALU.mult, ALU.add))
    s_rcmf = s_rcm[7 % 2]
    rcm_done = sv.v

    wait("scalar", sv, a_carry_v)
    wait("scalar", sw, a_done_w)
    q_w_done = [0, 0]
    q_pe_done = [0, 0]
    b_bank_free = {}
    kvt_done = 0
    for ke in range(KD):
        p = ke % 2
        wait("gpsimd", pe, q_pe_done[p])
        dma("gpsimd", sgq[p], s_wq[p][:], wq[ke])
        q_w_done[p] = sgq[p].v
        wait("tensor", sgq[p], q_w_done[p])
        wait("scalar", sv, sv.v)
        wait("scalar", srk, srk.v)
        for g in range(GT):
            bank = g % 2
            if bank in b_bank_free:
                wait("tensor", sa, b_bank_free[bank])
            for kd in range(KD):
                fn = (lambda e, p=p, kd=kd, g=g, bank=bank: e.matmul(
                    ps_b[bank][:], s_wq[p][:, kd * 128:(kd + 1) * 128],
                    s_big[:, kd * T + g * NT: kd * T + g * NT + NT],
                    start=(kd == 0), stop=(kd == KD - 1)))
                if kd == KD - 1:
                    op("tensor", pe, fn)
                else:
                    raw("tensor", fn)
            mm_done = pe.v
            gs = slice(g * NT, g * NT + NT)
            wait("scalar", pe, mm_done)
            op("scalar", sa, lambda e, bank=bank, gs=gs, ke=ke: e.activation(
                b1[:, gs], ps_b[bank][:], ACT.Identity, bias=s_bqkv[:, ke:ke + 1]))
            b_bank_free[bank] = sa.v
            wait("vector", sa, sa.v)
            op("vector", sv, lambda e, gs=gs: e.tensor_single_scalar(b2[:, gs], b1[:, gs], 0.0, ALU.min))
            wait("scalar", sv, sv.v)
            op("scalar", sa, lambda e, gs=gs: e.activation(b3[:, gs], b2[:, gs], ACT.Exp))
            wait("vector", sa, sa.v)
            op("vector", sv, lambda e, gs=gs: e.scalar_tensor_tensor(
                b2[:, gs], b1[:, gs], 0.0, b3[:, gs], ALU.max, ALU.add))
        phiq_done = sv.v
        wait("sync", sv, max(kvt_done, phiq_done))
        wait("sync", sa, sa.v)
        wait("sync", sw, a_done_w)
        dma("sync", srk, b3[:], d_kvs[ke])
        kvs_rb = srk.v
        wait("vector", srk, kvs_rb)
        wait("vector", sv, rcm_done)
        op("vector", sv, lambda e, ke=ke: e.tensor_scalar(
            b1[:], b3[:], s_rcmf[:, ke:ke + 1], None, ALU.add))
        kvt_done = sv.v
        op("vector", sv, lambda e, ke=ke: e.tensor_tensor(
            s_ctx[:, ke * T:(ke + 1) * T], b2[:], b1[:], ALU.mult))
        q_pe_done[p] = pe.v
        for _ in range(6):
            w1_task()
    ctx_done = sv.v
    while w1_idx[0] < len(w1_tiles):
        w1_task()

    wsum_v = sv.v
    wait("tensor", sv, wsum_v)
    for bank in list(b_bank_free):
        wait("tensor", sa, b_bank_free[bank])
    op("tensor", pe, lambda e: e.matmul(ps_b[0][0:1, 0:1], s_wsum[:, 0:1], s_one[:], start=True, stop=True))
    op("tensor", pe, lambda e: e.matmul(ps_b[0][0:1, 1:2], s_wsum[:, 1:2], s_one[:], start=True, stop=True))
    t_ws = pe.v
    wait("vector", pe, t_ws)
    op("vector", sv, lambda e: e.tensor_copy(s_red[0:1, 0:1], ps_b[0][0:1, 0:1]))
    op("vector", sv, lambda e: e.tensor_copy(s_red2[0:1, 0:1], ps_b[0][0:1, 1:2]))
    tot_v = sv.v
    wait("tensor", sv, tot_v)
    op("tensor", pe, lambda e: e.matmul(ps_b[1][:, 0:1], s_row[:], s_red[0:1, 0:1], start=True, stop=True))
    op("tensor", pe, lambda e: e.matmul(ps_b[1][:, 1:2], s_row[:], s_red2[0:1, 0:1], start=True, stop=True))
    t_bc = pe.v
    wait("vector", pe, t_bc)
    op("vector", sv, lambda e: e.tensor_scalar(s_swu[:], ps_b[1][:, 0:1], 1.0 / (D * H), None, ALU.mult))
    op("vector", sv, lambda e: e.tensor_scalar(s_swd[:], ps_b[1][:, 1:2], 1.0 / (H * D), None, ALU.mult))
    op("vector", sv, lambda e: e.tensor_scalar(s_swu[:], s_swu[:], EPS, None, ALU.max))
    op("vector", sv, lambda e: e.tensor_scalar(s_swd[:], s_swd[:], EPS, None, ALU.max))
    op("vector", sv, lambda e: e.reciprocal(s_iwu[:], s_swu[:]))
    op("vector", sv, lambda e: e.reciprocal(s_iwd[:], s_swd[:]))
    w2_ready[0] = sv.v

    g_ab.close()
    g_o = ExitStack()
    s_x2 = [mk_sb(g_o, f"s_x2{i}", [128, D]) for i in range(2)]
    s_sq2 = [mk_sb(g_o, f"s_sq2{i}", [128, D]) for i in range(2)]
    s_r1 = [mk_sb(g_o, f"s_r1{i}", [128, D]) for i in range(2)]
    s_nup = [mk_sb(g_o, f"s_nup{i}", [128, D], BF16) for i in range(2)]
    s_bo = mk_sb(g_o, "s_bo", [128, D])

    ab_v, ab_a = sv.v, sa.v
    wait("sync", sv, ab_v)
    wait("sync", sa, ab_a)
    wait("gpsimd", pe, pe.v)
    wait("gpsimd", sv, ab_v)
    dma("sync", sr, s_bo[:], bo_rep)
    for ke in range(KD):
        dma("gpsimd", sg, s_big[:, ke * T:(ke + 1) * T], wo[ke])
    wo_done = sg.v
    wait("tensor", sg, wo_done)
    wait("tensor", sr, sr.v)
    wait("tensor", sv, ctx_done)
    wait("vector", sr, sr.v)
    o_bank_free = {}
    r1_built = [0, 0]
    nup_done = [0, 0]
    nupw_done = [0, 0]
    r1w_done = [0, 0]
    x2_done = [0, 0]
    for t in range(KT):
        p = t % 2
        wait("sync", sv, r1_built[p])
        wait("sync", sa, nup_done[p])
        wait("sync", sx[p], sx[p].v)
        dma("sync", sx[p], s_x2[p][:], x[t * 128:(t + 1) * 128, :])
        x2_done[p] = sx[p].v
        for eo in range(GE):
            bank = p
            if bank in o_bank_free:
                wait("tensor", sv, o_bank_free[bank])
            for e in range(KD):
                fn = (lambda e_, e=e, t=t, eo=eo, bank=bank: e_.matmul(
                    ps_o[bank][:], s_ctx[:, e * T + t * 128: e * T + t * 128 + 128],
                    s_big[:, e * T + eo * NT: e * T + eo * NT + NT],
                    start=(e == 0), stop=(e == KD - 1)))
                if e == KD - 1:
                    op("tensor", pe, fn)
                else:
                    raw("tensor", fn)
            mm_done = pe.v
            es = slice(eo * NT, eo * NT + NT)
            wait("vector", pe, mm_done)
            wait("vector", sx[p], x2_done[p])
            wait("vector", sa, nup_done[p])
            wait("vector", swr[p], r1w_done[p])
            op("vector", sv, lambda e, bank=bank, es=es, p=p: e.scalar_tensor_tensor(
                s_sq2[p][:, es], ps_o[bank][:], SCALE, s_bo[:, es], ALU.mult, ALU.add))
            o_bank_free[bank] = sv.v
            op("vector", sv, lambda e, es=es, p=p: e.tensor_tensor(
                s_r1[p][:, es], s_sq2[p][:, es], s_x2[p][:, es], ALU.add))
        r1_built[p] = sv.v
        tc = slice(t, t + 1)
        op("vector", sv, lambda e, p=p: e.tensor_tensor(s_sq2[p][:], s_r1[p][:], s_r1[p][:], ALU.mult))
        op("vector", sv, lambda e, p=p, tc=tc: e.tensor_reduce(s_ssq[:, tc], s_sq2[p][:], AX.X, ALU.add))
        op("vector", sv, lambda e, p=p, tc=tc: e.tensor_reduce(s_sum[:, tc], s_r1[p][:], AX.X, ALU.add))
        op("vector", sv, lambda e, tc=tc: e.tensor_scalar(s_sum[:, tc], s_sum[:, tc], inv_d, None, ALU.mult))
        op("vector", sv, lambda e, tc=tc: e.tensor_scalar(s_ssq[:, tc], s_ssq[:, tc], inv_d, None, ALU.mult))
        op("vector", sv, lambda e, tc=tc: e.tensor_tensor(s_sc1[:, tc], s_sum[:, tc], s_sum[:, tc], ALU.mult))
        op("vector", sv, lambda e, tc=tc: e.tensor_tensor(s_sc1[:, tc], s_ssq[:, tc], s_sc1[:, tc], ALU.subtract))
        op("vector", sv, lambda e, tc=tc: e.tensor_scalar(s_sc1[:, tc], s_sc1[:, tc], EPS, None, ALU.add))
        op("vector", sv, lambda e, tc=tc: e.reciprocal(s_sc2[:, tc], s_sc1[:, tc]))
        wait("scalar", sv, sv.v)
        op("scalar", sa, lambda e, tc=tc: e.activation(s_rstd[:, tc], s_sc2[:, tc], ACT.Sqrt))
        wait("vector", sa, sa.v)
        newton(s_rstd[:, tc], s_sc1[:, tc], s_sc2[:, tc])
        op("vector", sv, lambda e, tc=tc: e.tensor_tensor(s_nmr[:, tc], s_sum[:, tc], s_rstd[:, tc], ALU.mult))
        op("vector", sv, lambda e, tc=tc: e.tensor_scalar(s_nmr[:, tc], s_nmr[:, tc], -1.0, None, ALU.mult))
        wait("scalar", sv, sv.v)
        op("scalar", sa, lambda e, p=p, tc=tc: e.activation(
            s_sq2[p][:], s_r1[p][:], ACT.Identity, bias=s_nmr[:, tc], scale=s_rstd[:, tc]))
        wait("vector", sa, sa.v)
        op("vector", sv, lambda e, p=p, tc=tc: e.tensor_reduce(
            s_c2[:, tc], s_sq2[p][:], AX.X, ALU.max, apply_absolute_value=True))
        op("vector", sv, lambda e, tc=tc: e.tensor_scalar(s_c2[:, tc], s_c2[:, tc], EPS, None, ALU.max))
        op("vector", sv, lambda e, tc=tc: e.reciprocal(s_sc2[:, tc], s_c2[:, tc]))
        op("vector", sv, lambda e, tc=tc: e.tensor_scalar(s_s2[:, tc], s_sc2[:, tc], 127.0, None, ALU.mult))
        op("vector", sv, lambda e, p=p, tc=tc: e.tensor_scalar(
            s_sq2[p][:], s_sq2[p][:], s_s2[:, tc], MAGIC, ALU.mult, ALU.add))
        wait("scalar", sv, sv.v)
        wait("scalar", swn[p], nupw_done[p])
        op("scalar", sa, lambda e, p=p: e.activation(s_nup[p][:], s_sq2[p][:], ACT.Identity, bias=s_nmg[:, 0:1]))
        nup_done[p] = sa.v
        wait("sync", sa, nup_done[p])
        dma("sync", swn[p], d_nup[t * 128:(t + 1) * 128, :], s_nup[p][:])
        nupw_done[p] = swn[p].v
        wait("sync", sv, r1_built[p])
        dma("sync", swr[p], d_r1[t * 128:(t + 1) * 128, :], s_r1[p][:])
        r1w_done[p] = swr[p].v
        for _ in range(8):
            w2_task()

    g_o.close()
    g_up = ExitStack()
    s_wup = [mk_sb(g_up, f"s_wup{i}", [128, KD * NT], BF16) for i in range(2)]
    s_rel = [mk_sb(g_up, f"s_rel{i}", [128, NT]) for i in range(2)]

    o_v, o_a = sv.v, sa.v
    wait("sync", swn[0], swn[0].v)
    wait("sync", swn[1], swn[1].v)
    wait("sync", swr[0], swr[0].v)
    wait("sync", swr[1], swr[1].v)
    wait("sync", pe, pe.v)
    wait("sync", sv, o_v)
    wait("sync", sa, o_a)
    wait("sync", sx[0], sx[0].v)
    wait("sync", sx[1], sx[1].v)
    wait("gpsimd", sv, o_v)
    wait("gpsimd", sa, o_a)
    wait("gpsimd", swr[0], swr[0].v)
    wait("gpsimd", swr[1], swr[1].v)
    for kd in range(KD):
        dma("sync", sr, s_ctx[:, kd * T:(kd + 1) * T],
            d_nup[:, kd * 128:(kd + 1) * 128], transpose=True)
    nupT_done = sr.v
    wait("tensor", sr, nupT_done)
    up_pe_done = [0, 0]
    upw_done = [0, 0]
    up_bank_free = {}
    relw_done = [0, 0]
    up_v_read = [0, 0]
    for h in range(GH):
        p = h % 2
        while w2_idx[0] < len(w2_tiles) and w2_tiles[w2_idx[0]][0] == "u":
            w2_task()
        wait("gpsimd", qsu[0], qsu[0].v)
        wait("gpsimd", qsu[1], qsu[1].v)
        wait("gpsimd", pe, up_pe_done[p])
        for kd in range(KD):
            dma("gpsimd", sgu[p], s_wup[p][:, kd * NT:(kd + 1) * NT],
                d_wupq[kd][:, h * NT:(h + 1) * NT])
        upw_done[p] = sgu[p].v
        wait("tensor", sgu[p], upw_done[p])
        for t in range(KT):
            bank = t % 2
            if bank in up_bank_free:
                wait("tensor", sa, up_bank_free[bank])
            for kd in range(KD):
                fn = (lambda e, kd=kd, t=t, p=p, bank=bank: e.matmul(
                    ps_a[bank][:], s_ctx[:, kd * T + t * 128: kd * T + t * 128 + 128],
                    s_wup[p][:, kd * NT:(kd + 1) * NT],
                    start=(kd == 0), stop=(kd == KD - 1)))
                if kd == KD - 1:
                    op("tensor", pe, fn)
                else:
                    raw("tensor", fn)
            mm_done = pe.v
            wait("scalar", pe, mm_done)
            wait("scalar", swl[bank], relw_done[bank])
            wait("scalar", sv, up_v_read[bank])
            op("scalar", sa, lambda e, bank=bank: e.activation(s_rel[bank][:], ps_a[bank][:], ACT.Relu))
            up_bank_free[bank] = sa.v
            wait("vector", sa, sa.v)
            wait("vector", sv, sv.v)
            op("vector", sv, lambda e, bank=bank: e.tensor_reduce(s_red[:], s_rel[bank][:], AX.X, ALU.max))
            op("vector", sv, lambda e, t=t: e.tensor_tensor(s_maxr[:, t:t + 1], s_maxr[:, t:t + 1], s_red[:], ALU.max))
            up_v_read[bank] = sv.v
            wait("sync", sa, up_bank_free[bank])
            dma("sync", swl[bank], d_relu[t * 128:(t + 1) * 128, h * NT:(h + 1) * NT], s_rel[bank][:])
            relw_done[bank] = swl[bank].v
        up_pe_done[p] = pe.v
        for _ in range(16):
            w2_task()
    up_done_v = sv.v
    while w2_idx[0] < len(w2_tiles):
        w2_task()

    wait("vector", sv, up_done_v)
    op("vector", sv, lambda e: e.tensor_scalar(s_k2[:], s_c2[:], s_swu[:, 0:1], 1.0 / 127.0, ALU.mult, ALU.mult))
    op("vector", sv, lambda e: e.tensor_tensor(s_sc1[:], s_maxr[:], s_k2[:], ALU.mult))
    op("vector", sv, lambda e: e.tensor_tensor(s_sc1[:], s_sc1[:], s_sc1[:], ALU.mult))
    op("vector", sv, lambda e: e.tensor_scalar(s_sc1[:], s_sc1[:], EPS, None, ALU.max))
    op("vector", sv, lambda e: e.reciprocal(s_sc2[:], s_sc1[:]))
    op("vector", sv, lambda e: e.tensor_tensor(s_q3[:], s_k2[:], s_k2[:], ALU.mult))
    op("vector", sv, lambda e: e.tensor_scalar(s_q3[:], s_q3[:], 127.0, None, ALU.mult))
    op("vector", sv, lambda e: e.tensor_tensor(s_q3[:], s_q3[:], s_sc2[:], ALU.mult))
    op("vector", sv, lambda e: e.tensor_scalar(s_k3[:], s_sc1[:], s_swd[:, 0:1], 1.0 / 127.0, ALU.mult, ALU.mult))
    q3_done = sv.v

    g_up.close()
    g_pw.close()
    g_nd = ExitStack()
    s_rlb = [mk_sb(g_nd, f"s_rlb{i}", [128, HH]) for i in range(2)]
    s_sqh = [mk_sb(g_nd, f"s_sqh{i}", [128, HH]) for i in range(2)]
    s_ndn = [mk_sb(g_nd, f"s_ndn{i}", [128, HH], BF16) for i in range(2)]

    up_a = sa.v
    wait("sync", swl[0], swl[0].v)
    wait("sync", swl[1], swl[1].v)
    wait("sync", sa, up_a)
    wait("sync", sv, sv.v)
    wait("sync", ql[0], ql[0].v)
    wait("sync", ql[1], ql[1].v)
    wait("sync", qsu[0], qsu[0].v)
    wait("sync", qsu[1], qsu[1].v)
    wait("sync", qsd[0], qsd[0].v)
    wait("sync", qsd[1], qsd[1].v)
    wait("sync", sgu[0], sgu[0].v)
    wait("sync", sgu[1], sgu[1].v)
    nd_v_done = [0, 0]
    nd_a_done = [0, 0]
    nd_w_done = [0, 0]
    idx = 0
    for t in range(KT):
        for hhf in range(2):
            p = idx % 2
            idx += 1
            wait("sync", sv, nd_v_done[p])
            wait("sync", sa, nd_a_done[p])
            dma("sync", srl[p], s_rlb[p][:], d_relu[t * 128:(t + 1) * 128, hhf * HH:(hhf + 1) * HH])
            rb = srl[p].v
            wait("vector", srl[p], rb)
            wait("vector", sv, q3_done)
            wait("vector", sa, nd_a_done[p])
            op("vector", sv, lambda e, p=p: e.tensor_tensor(s_sqh[p][:], s_rlb[p][:], s_rlb[p][:], ALU.mult))
            op("vector", sv, lambda e, p=p, t=t: e.tensor_scalar(
                s_sqh[p][:], s_sqh[p][:], s_q3[:, t:t + 1], MAGIC, ALU.mult, ALU.add))
            nd_v_done[p] = sv.v
            wait("scalar", sv, nd_v_done[p])
            wait("scalar", swd[p], nd_w_done[p])
            op("scalar", sa, lambda e, p=p: e.activation(s_ndn[p][:], s_sqh[p][:], ACT.Identity, bias=s_nmg[:, 0:1]))
            nd_a_done[p] = sa.v
            wait("sync", sa, nd_a_done[p])
            dma("sync", swd[p], d_ndn[t * 128:(t + 1) * 128, hhf * HH:(hhf + 1) * HH], s_ndn[p][:])
            nd_w_done[p] = swd[p].v

    g_nd.close()
    g_dn = ExitStack()
    s_r1L = [mk_sb(g_dn, f"s_r1L{i}", [128, NT]) for i in range(2)]
    s_o = [mk_sb(g_dn, f"s_o{i}", [128, NT]) for i in range(2)]
    assert KH * TH == KD * T and 2 * KH * NT <= KD * T

    nd_v, nd_a = sv.v, sa.v
    up_pe_total = pe.v
    wait("gpsimd", pe, up_pe_total)
    dn_pe_done = [0, 0]
    dn_o_guard = [0, 0]
    dn_r_guard = [0, 0]
    dn_bank_free = {}
    for tc in range(2):
        wait("sync", swd[0], swd[0].v)
        wait("sync", swd[1], swd[1].v)
        wait("sync", swr[0], swr[0].v)
        wait("sync", swr[1], swr[1].v)
        wait("sync", pe, pe.v)
        wait("sync", sv, nd_v)
        wait("sync", sa, nd_a)
        for kh in range(KH):
            dma("sync", sr, s_big[:, kh * TH:(kh + 1) * TH],
                d_ndn[tc * TH:(tc + 1) * TH, kh * 128:(kh + 1) * 128], transpose=True)
        ndT_done = sr.v
        wait("tensor", sr, ndT_done)
        for e in range(GE):
            pwe = e % 2
            wait("gpsimd", qsd[0], qsd[0].v)
            wait("gpsimd", qsd[1], qsd[1].v)
            wait("gpsimd", pe, dn_pe_done[pwe])
            wait("gpsimd", sv, nd_v)
            for kh in range(KH):
                dma("gpsimd", sgd[pwe], s_ctx[:, (pwe * KH + kh) * NT:(pwe * KH + kh + 1) * NT],
                    d_wdnq[kh][:, e * NT:(e + 1) * NT])
            dnw = sgd[pwe].v
            wait("tensor", sgd[pwe], dnw)
            for ti in range(KTH):
                t = tc * KTH + ti
                bank = 2 + (ti % 2)
                if bank in dn_bank_free:
                    wait("tensor", sv, dn_bank_free[bank])
                for kh in range(KH):
                    fn = (lambda e_, kh=kh, ti=ti, pwe=pwe, bank=bank: e_.matmul(
                        ps_a[bank][:], s_big[:, kh * TH + ti * 128: kh * TH + ti * 128 + 128],
                        s_ctx[:, (pwe * KH + kh) * NT:(pwe * KH + kh + 1) * NT],
                        start=(kh == 0), stop=(kh == KH - 1)))
                    if kh == KH - 1:
                        op("tensor", pe, fn)
                    else:
                        raw("tensor", fn)
                mm_done = pe.v
                pp = ti % 2
                wait("sync", sv, dn_r_guard[pp])
                dma("sync", srr[pp], s_r1L[pp][:], d_r1[t * 128:(t + 1) * 128, e * NT:(e + 1) * NT])
                r1rb = srr[pp].v
                wait("vector", srr[pp], r1rb)
                wait("vector", pe, mm_done)
                wait("vector", so, dn_o_guard[pp])
                op("vector", sv, lambda e_, bank=bank, pp=pp, t=t: e_.scalar_tensor_tensor(
                    s_o[pp][:], ps_a[bank][:], s_k3[:, t:t + 1], s_r1L[pp][:], ALU.mult, ALU.add))
                dn_bank_free[bank] = sv.v
                dn_r_guard[pp] = sv.v
                wait("sync", sv, sv.v)
                dma("sync", so, out[t * 128:(t + 1) * 128, e * NT:(e + 1) * NT], s_o[pp][:])
                dn_o_guard[pp] = so.v
            dn_pe_done[pwe] = pe.v
    wait("sync", so, so.v)
    g_dn.close()

    with ctx:
        block = ctx.enter_context(nc.Block())

        @block.sync
        def _(eng):
            for f in plan["sync"]:
                f(eng)

        @block.gpsimd
        def _(eng):
            for f in plan["gpsimd"]:
                f(eng)

        @block.vector
        def _(eng):
            for f in plan["vector"]:
                f(eng)

        @block.scalar
        def _(eng):
            for f in plan["scalar"]:
                f(eng)

        @block.tensor
        def _(eng):
            for f in plan["tensor"]:
                f(eng)

    return nc


def _lay_proj(W):
    E, D = W.shape
    KD = D // 128
    return np.ascontiguousarray(W.T.reshape(KD, 128, E))


def _lay_stat(W):
    E, D = W.shape
    KE, KD = E // 128, D // 128
    A = W.reshape(KE, 128, KD, 128)
    return np.ascontiguousarray(A.transpose(0, 3, 2, 1).reshape(KE, 128, D))


def make_inputs(x_full, Wq, bq, Wk, bk, Wv, bv, Wo, bo, w_up, w_down, T):
    D = Wq.shape[0]
    KD = D // 128
    wq_l = _lay_stat(np.asarray(Wq, np.float32))
    wk_l = _lay_stat(np.asarray(Wk, np.float32))
    wv_l = _lay_stat(np.asarray(Wv, np.float32))
    wo_l = _lay_proj(np.asarray(Wo, np.float32))
    wup_l = _lay_proj(np.asarray(w_up, np.float32))
    wdn_l = _lay_proj(np.asarray(w_down, np.float32))
    bqkv = np.concatenate([np.asarray(bq, np.float32).reshape(KD, 128).T,
                           np.asarray(bk, np.float32).reshape(KD, 128).T,
                           np.asarray(bv, np.float32).reshape(KD, 128).T], axis=1)
    bo_r = np.ascontiguousarray(np.broadcast_to(
        np.asarray(bo, np.float32)[None, :], (128, D)))
    maps = []
    for c in range(8):
        ps = np.zeros((128, 8), np.float32)
        if c % 2 == 1:
            ps[:, c - 1] = 1.0
        maps.append({
            "x": np.ascontiguousarray(x_full[c * T:(c + 1) * T]),
            "wq": wq_l, "wk": wk_l, "wv": wv_l, "wo": wo_l,
            "bqkv": np.ascontiguousarray(bqkv), "bo_rep": bo_r,
            "wup": wup_l, "wdn": wdn_l, "psel": ps,
        })
    return maps


def kernel(L0_attn_x, L0_attn_Wq, L0_attn_bq, L0_attn_Wk, L0_attn_bk,
           L0_attn_Wv, L0_attn_bv, L0_attn_Wo, L0_attn_bo,
           L0_ffn_w_up, L0_ffn_w_down):
    B, L, D = L0_attn_x.shape
    T = (B * L) // 8
    H = L0_ffn_w_up.shape[0]
    x_full = np.asarray(L0_attn_x, dtype=np.float32).reshape(B * L, D)
    nc = build_nc(D, T, H, NT=512)
    in_maps = make_inputs(x_full, L0_attn_Wq, L0_attn_bq, L0_attn_Wk, L0_attn_bk,
                          L0_attn_Wv, L0_attn_bv, L0_attn_Wo, L0_attn_bo,
                          L0_ffn_w_up, L0_ffn_w_down, T)
    res = run_bass_kernel_spmd(nc, in_maps, core_ids=list(range(8)))
    outs = [np.asarray(res.results[c]["out"]) for c in range(8)]
    return np.concatenate(outs, axis=0).reshape(B, L, D).astype(np.float32)
